# version 01 (final)
# baseline (speedup 1.0000x reference)
import numpy as np
import ml_dtypes
import contextlib
import concourse.bass as bass
import concourse.mybir as mybir
from concourse.bass_utils import run_bass_kernel_spmd

F32 = mybir.dt.float32
BF16 = mybir.dt.bfloat16
I32 = mybir.dt.int32
AF = mybir.ActivationFunctionType
ALU = mybir.AluOpType
AX = mybir.AxisListType

NCH = 8
NSL = 4
SL = 512
LN16 = 2.772588722239781

N_BLOCKS = 4
DEV_TAPS = False
DEV_TAPS2 = False
FFN_PHASE = 99
DEV_TAPS3 = False
PHASE = 99
SUB = 99


class Tok:
    __slots__ = ("sem", "val")
    def __init__(self, sem, val):
        self.sem = sem
        self.val = val


class K:
    ENG = ("tensor", "vector", "scalar", "gpsimd", "sync")

    def __init__(self, nc, stack):
        self.nc = nc
        self.b = {}
        self.eng = {n: getattr(nc, n) for n in self.ENG}
        self.sem = {n: stack.enter_context(nc.semaphore(f"sm_{n}")) for n in self.ENG}
        for q in ("in", "w0", "w1", "w2", "w3", "out"):
            self.sem["d_" + q] = stack.enter_context(nc.semaphore("sd_" + q))
        self.cnt = {n: 0 for n in self.sem}
        self.last = {n: None for n in self.ENG}
        self.last_tok = {n: None for n in self.ENG}
        self.hw = {}
        self.bank_guard = {}
        self.stage_guard = [None, None]

    def _wait(self, ename, toks):
        e = self.eng[ename]
        for t in toks:
            if t is None:
                continue
            key = (ename, t.sem)
            if self.hw.get(key, 0) >= t.val:
                continue
            e.wait_ge(self.sem[t.sem], t.val)
            self.hw[key] = t.val

    def on(self, ename, fn, after=()):
        self._wait(ename, after)
        ins = fn(self.eng[ename])
        self.last[ename] = ins
        self.last_tok[ename] = None
        return ins

    def tok(self, ename):
        if self.last_tok[ename] is not None:
            return self.last_tok[ename]
        assert self.last[ename] is not None
        self.cnt[ename] += 1
        self.last[ename].then_inc(self.sem[ename], 1)
        t = Tok(ename, self.cnt[ename])
        self.last_tok[ename] = t
        return t

    def dma(self, dst, src, after=(), sem="in"):
        self._wait("sync", after)
        if sem == "in" and self.cnt["d_in"] > 0:
            self._wait("sync", [Tok("d_in", self.cnt["d_in"])])
        ins = self.eng["sync"].dma_start(out=dst, in_=src)
        ds = "d_" + sem
        self.cnt[ds] += 16
        ins.then_inc(self.sem[ds], 16)
        self.last["sync"] = ins
        self.last_tok["sync"] = None
        return Tok(ds, self.cnt[ds])

    def bank(self, bid, after=()):
        t = self.bank_guard.get(bid)
        self._wait("tensor", list(after) + ([t] if t is not None else []))

    def bank_done(self, bid, evac_tok):
        self.bank_guard[bid] = evac_tok

    def barrier(self):
        toks = [Tok("d_" + q, self.cnt["d_" + q]) for q in ("in", "w0", "w1", "w2", "w3", "out")
                if self.cnt["d_" + q] > 0]
        for e in ("tensor", "vector", "scalar", "gpsimd"):
            if self.last[e] is not None:
                toks.append(self.tok(e))
        for e in ("tensor", "vector", "scalar", "gpsimd", "sync"):
            self._wait(e, toks)
        self.bank_guard = {}
        self.stage_guard = [None, None]
        return toks


def _prep_consts():
    c = {}
    c["c_ones_col_bf"] = np.ones((128, 1), dtype=ml_dtypes.bfloat16)
    c["c_ones_row_f32"] = np.ones((1, 128), dtype=np.float32)
    c["c_ones_row_bf"] = np.ones((1, 128), dtype=ml_dtypes.bfloat16)
    pair2 = np.zeros((2, 128), dtype=ml_dtypes.bfloat16)
    pair2[0, 0:64] = 1
    pair2[1, 64:128] = 1
    c["c_pair2_bf"] = pair2
    fold = np.zeros((128, 64), dtype=ml_dtypes.bfloat16)
    fold[np.arange(64), np.arange(64)] = 1
    fold[np.arange(64) + 64, np.arange(64)] = 1
    c["c_fold64_bf"] = fold
    c["c_eye_bf"] = np.eye(128, dtype=ml_dtypes.bfloat16)
    c["c_eye_f32"] = np.eye(128, dtype=np.float32)
    invk = np.tile(1.0 / np.arange(1, 33, dtype=np.float32), 16)
    c["c_invk"] = np.ascontiguousarray(np.broadcast_to(invk, (128, 512)))
    ii = np.arange(128)[:, None]
    ll = np.arange(512)[None, :]
    cm = np.stack([(ii <= ll - 128 * j) for j in range(4)])
    c["c_cmask"] = cm.astype(np.float32).astype(ml_dtypes.bfloat16)
    ee = np.arange(128)[:, None]
    cc = np.arange(128)[None, :]
    c["c_trimask"] = (cc >= ee).astype(np.float32).astype(ml_dtypes.bfloat16)
    return c


def _prep_core(core, inp):
    d = {}
    d["x_fm"] = np.ascontiguousarray(inp["L0_attn_query"][core].T)
    d["l0_wq"] = inp["L0_attn_Wq"]
    d["l0_wk"] = inp["L0_attn_Wk"]
    d["l0_wv"] = inp["L0_attn_Wv"]
    d["l0_wo"] = inp["L0_attn_Wo"]
    d["l0_bq"] = inp["L0_attn_bq"].reshape(1, 1024)
    d["l0_bv"] = inp["L0_attn_bv"].reshape(1, 1024)
    d["l0_bk_col"] = np.ascontiguousarray(inp["L0_attn_bk"].reshape(8, 128).T)
    d["l0_bv_col"] = np.ascontiguousarray(inp["L0_attn_bv"].reshape(8, 128).T)
    d["l0_bo_col"] = np.ascontiguousarray(inp["L0_attn_bo"].reshape(8, 128).T)
    kpT = inp["L0_attn_key_param"][0].T
    d["l0_kp"] = np.ascontiguousarray(
        kpT.reshape(8, 128, 32).transpose(1, 0, 2).reshape(128, 256))
    sim = inp["L0_ffn_sim_matrix"]
    d["l0f_sim_t"] = np.ascontiguousarray(
        sim.reshape(8, 128, 16).transpose(1, 0, 2).reshape(128, 128))
    d["l0f_gates"] = inp["L0_ffn_gates"].reshape(1, 16)
    for nm, key in (("qp", "L0_ffn_q_proj"), ("kp", "L0_ffn_k_proj"), ("vp", "L0_ffn_v_proj")):
        w = inp[key]
        pairs = np.stack([
            np.concatenate([w[2 * j], w[2 * j + 1]], axis=1)
              .reshape(8, 128, 128).transpose(1, 0, 2).reshape(128, 1024)
            for j in range(8)])
        d[f"l0f_{nm}"] = np.ascontiguousarray(pairs)
    op = inp["L0_ffn_o_proj"]
    d["l0f_op"] = np.ascontiguousarray(
        np.stack([np.concatenate([op[2 * j], op[2 * j + 1]], axis=0) for j in range(8)]))
    d["l1_wqT"] = np.ascontiguousarray(inp["L1_attn_Wq"].T)
    d["l1_wkT"] = np.ascontiguousarray(inp["L1_attn_Wk"].T)
    d["l1_wvT"] = np.ascontiguousarray(inp["L1_attn_Wv"].T)
    d["l1_woT"] = np.ascontiguousarray(inp["L1_attn_Wo"].T)
    d["l1_fmqwT"] = np.ascontiguousarray(inp["L1_attn_fmq_w"].T)
    d["l1_fmkwT"] = np.ascontiguousarray(inp["L1_attn_fmk_w"].T)
    d["l1_fmqb_col"] = inp["L1_attn_fmq_b"].reshape(128, 1)
    d["l1_fmkb_col"] = inp["L1_attn_fmk_b"].reshape(128, 1)
    d["l1_fmkb_row"] = inp["L1_attn_fmk_b"].reshape(1, 128)
    d["l1_rmsw_col"] = inp["L1_attn_rms_w"].reshape(128, 1)
    wgT = inp["L1_ffn_w_gate"].T
    wuT = inp["L1_ffn_w_up"].T
    wdT = inp["L1_ffn_w_down"].T
    d["l1f_wg"] = np.ascontiguousarray(wgT.reshape(8, 128, 24, 128).transpose(2, 0, 1, 3))
    d["l1f_wu"] = np.ascontiguousarray(wuT.reshape(8, 128, 24, 128).transpose(2, 0, 1, 3))
    d["l1f_wd"] = np.ascontiguousarray(wdT.reshape(24, 128, 8, 128).transpose(2, 0, 1, 3))
    for kk, v in _prep_consts().items():
        d[kk] = v.view(np.uint16) if v.dtype == ml_dtypes.bfloat16 else v
    return d


def emit_stats(k, xb, xb_toks, sq, negmean_bf=None, std_bf=None):
    b = k.b
    ones_bf = b["ones_col_bf"][:]
    pbA = b["pb"][6][0:1, :]
    pbB = b["pb"][5][0:1, :]
    mean, msvar, rstd = b["row_mean"], b["row_msvar"], b["row_rstd"]
    toks = {}
    sq_guard = [None, None]
    evac_tok = None
    for s in range(NSL):
        sl = slice(s * SL, (s + 1) * SL)
        for c in range(NCH):
            sqt = sq[c % 2]
            k.on("scalar", lambda e, c=c, sl=sl, sqt=sqt: e.activation(
                sqt[:], xb[c][:, sl], AF.Square, bias=0.0, scale=1.0),
                after=[xb_toks[c], sq_guard[c % 2]])
            tsq = k.tok("scalar")
            if c == 0:
                k.bank("st6", after=[evac_tok])
                k.bank("st5", after=[evac_tok])
            k.on("tensor", lambda e, c=c, sl=sl: e.matmul(
                pbA, ones_bf, xb[c][:, sl], start=(c == 0), stop=(c == NCH - 1)),
                after=[xb_toks[c]])
            k.on("tensor", lambda e, c=c, sqt=sqt: e.matmul(
                pbB, ones_bf, sqt[:], start=(c == 0), stop=(c == NCH - 1)),
                after=[tsq])
            sq_guard[c % 2] = k.tok("tensor")
        tmm = k.tok("tensor")
        k.on("scalar", lambda e, sl=sl: e.activation(
            mean[0:1, sl], pbA, AF.Copy, bias=0.0, scale=1.0 / 1024.0), after=[tmm])
        k.on("scalar", lambda e, sl=sl: e.activation(
            msvar[0:1, sl], pbB, AF.Copy, bias=0.0, scale=1.0 / 1024.0), after=[tmm])
        evac_tok = k.tok("scalar")
    trows = evac_tok
    k.on("vector", lambda e: e.tensor_tensor(
        rstd[0:1, :], mean[0:1, :], mean[0:1, :], ALU.mult), after=[trows])
    k.on("vector", lambda e: e.tensor_tensor(
        msvar[0:1, :], msvar[0:1, :], rstd[0:1, :], ALU.subtract))
    tvar = k.tok("vector")
    k.on("scalar", lambda e: e.activation(
        rstd[0:1, :], msvar[0:1, :], AF.Sqrt, bias=b["eps_col"][0:1, 0:1], scale=1.0),
        after=[tvar])
    tstd = k.tok("scalar")
    if std_bf is not None:
        k.on("scalar", lambda e: e.activation(
            std_bf[0:1, :], rstd[0:1, :], AF.Copy, bias=0.0, scale=1.0))
        toks["std"] = k.tok("scalar")
        tstd = toks["std"]
    k.on("vector", lambda e: e.reciprocal(rstd[0:1, :], rstd[0:1, :]), after=[tstd])
    toks["rstd"] = k.tok("vector")
    if negmean_bf is not None:
        k.on("vector", lambda e: e.tensor_scalar(
            negmean_bf[0:1, :], mean[0:1, :], -1.0, None, ALU.mult), after=[trows])
        toks["negmean"] = k.tok("vector")
    toks["var"] = tvar
    toks["rows"] = trows
    toks["bank56_free"] = trows
    return toks


def emit_replicate(k, dst, src_row, bid, after, n_parts=128):
    b = k.b
    pbk = b["pb"][bid]
    tk = None
    width = src_row.shape[-1]
    for s in range((width + SL - 1) // SL):
        sl = slice(s * SL, min((s + 1) * SL, width))
        w = sl.stop - sl.start
        k.bank(bid, after=after)
        k.on("tensor", lambda e, sl=sl, w=w: e.matmul(
            pbk[0:n_parts, 0:w], b["ones_row_f32"][0:1, 0:n_parts], src_row[0:1, sl],
            start=True, stop=True))
        tmm = k.tok("tensor")
        k.on("scalar", lambda e, sl=sl, w=w: e.activation(
            dst[0:n_parts, sl], pbk[0:n_parts, 0:w], AF.Copy, bias=0.0, scale=1.0),
            after=[tmm])
        tk = k.tok("scalar")
        k.bank_done(bid, tk)
    return tk


SORT_STAGES = []
def _sort_stages():
    global SORT_STAGES
    if not SORT_STAGES:
        st = []
        w = 2
        while w <= 32:
            st.append(("rev", w))
            d = w // 4
            while d >= 1:
                st.append(("clean", d))
                d //= 2
            w *= 2
        SORT_STAGES = st
    return SORT_STAGES


def emit_sparsemax(k, z, zs_a, zs_b, tau, a_bf, after_z):
    b = k.b
    if SUB < 3:
        return None
    src = z
    dst_list = [zs_a, zs_b]
    di = 0
    t_prev = list(after_z)
    for kind, w in _sort_stages():
        dst = dst_list[di]
        if kind == "rev":
            lo_s = src[:].rearrange("p (b i) -> p b i", i=w)[:, :, 0:w // 2]
            if w == 2:
                hi_s = src[:].rearrange("p (b i) -> p b i", i=w)[:, :, 1:2]
            else:
                hi_s = src[:].rearrange("p (b i) -> p b i", i=w)[:, :, w - 1:w // 2 - 1:-1]
            lo_d = dst[:].rearrange("p (b i) -> p b i", i=w)[:, :, 0:w // 2]
            hi_d = dst[:].rearrange("p (b i) -> p b i", i=w)[:, :, w // 2:w]
        else:
            v_s = src[:].rearrange("p (b o i) -> p b o i", o=2, i=w)
            v_d = dst[:].rearrange("p (b o i) -> p b o i", o=2, i=w)
            lo_s, hi_s = v_s[:, :, 0, :], v_s[:, :, 1, :]
            lo_d, hi_d = v_d[:, :, 0, :], v_d[:, :, 1, :]
        k.on("vector", lambda e, a=lo_s, c=hi_s, o=lo_d: e.tensor_tensor(o, a, c, ALU.max),
             after=t_prev)
        k.on("vector", lambda e, a=lo_s, c=hi_s, o=hi_d: e.tensor_tensor(o, a, c, ALU.min))
        t_prev = [k.tok("vector")]
        src = dst
        di ^= 1
    sorted_buf = src
    cum = dst_list[di]
    if SUB < 4:
        return None
    for h in range(16):
        hs = slice(h * 32, (h + 1) * 32)
        k.on("vector", lambda e, hs=hs: e.tensor_tensor_scan(
            cum[:, hs], sorted_buf[:, hs], sorted_buf[:, hs], 0.0, ALU.add, ALU.bypass))
    if SUB < 5:
        return None
    k.on("vector", lambda e: e.scalar_tensor_tensor(
        sorted_buf[:], cum[:], -1.0, b["invk"][:], ALU.add, ALU.mult),
        after=[b.get("invk_tok")])
    k.on("vector", lambda e: e.reduce_max(
        tau[:], sorted_buf[:].rearrange("p (g s) -> p g s", g=16), axis=AX.X))
    k.on("vector", lambda e: e.tensor_tensor(
        cum[:].rearrange("p (g s) -> p g s", g=16),
        z[:].rearrange("p (g s) -> p g s", g=16),
        tau[:].unsqueeze(2).broadcast_to([128, 16, 32]), ALU.subtract))
    tsub = k.tok("vector")
    k.on("scalar", lambda e: e.activation(a_bf[:], cum[:], AF.Relu, bias=0.0, scale=1.0),
         after=[tsub])
    return k.tok("scalar")


def stream_wfull(k, dram_ap, wfull, after=(), scale=None):
    b = k.b
    toks = []
    for c in range(NCH):
        stg = b["wstage"][c % 2]
        dtok = k.dma(stg[:], dram_ap[c * 128:(c + 1) * 128, :],
                     after=[k.stage_guard[c % 2]] + list(after), sem=f"w{c % 2}")
        if scale is None:
            k.on("gpsimd", lambda e, c=c, stg=stg: e.tensor_copy(wfull[c][:], stg[:]),
                 after=[dtok] + list(after))
        else:
            k.on("gpsimd", lambda e, c=c, stg=stg: e.tensor_scalar(
                wfull[c][:], stg[:], scale, None, ALU.mult), after=[dtok] + list(after))
        k.stage_guard[c % 2] = k.tok("gpsimd")
        toks.append(k.stage_guard[c % 2])
    return toks


def block_l0attn(k, xr, xr_toks, scope):
    nc, b = k.nc, k.b
    pb = b["pb"]
    sb = scope

    xb = [sb(f"a_xb{c}", (128, 2048), BF16) for c in range(NCH)]
    b["invk"] = sb("a_invk", (128, 512), F32)
    b["invk_tok"] = k.dma(b["invk"][:], b["in"]["c_invk"])
    sq = [sb(f"a_sq{i}", (128, 512), BF16) for i in range(2)]
    negmean_bf = sb("a_negmean", (1, 2048), BF16)
    std_bf = sb("a_std", (1, 2048), BF16)
    rstd_rep = sb("a_rstdrep", (128, 2048), F32)
    wfull = [sb(f"a_wf{c}", (128, 1024), BF16) for c in range(NCH)]
    wchunk = [sb("a_wc0", (128, 1024), BF16)] * 2
    bq_bf = sb("a_bq", (1, 1024), BF16)
    wsum_bf = sb("a_wsum", (1, 1024), BF16)
    brow = b["wstage"][0][0:1, :]
    bcols = sb("a_bcols", (128, 24), F32)
    keys_bf = sb("a_keys", (128, 256), BF16)
    keys_pad = sb("a_keyspad", (128, 512), BF16)
    kp_bf = keys_pad[:, 0:256]
    bv_bf = sb("a_bv", (1, 1024), BF16)
    vals_sb = sb("a_valssb", (32, 1024), BF16)
    qsl = [sb(f"a_qs{c}", (128, 512), BF16) for c in range(NCH)]
    oall = qsl
    z = sb("a_z", (128, 512), F32)
    zs_a = sb("a_zsa", (128, 512), F32)
    zs_b = sb("a_zsb", (128, 512), F32)
    tau = sb("a_tau", (128, 16), F32)
    a_bf = sq[0]
    aT = [sb(f"a_aT{j}", (128, 2048), BF16) for j in range(4)]

    k.dma(bcols[:, 0:8], b["in"]["l0_bk_col"])
    k.dma(bcols[:, 8:16], b["in"]["l0_bo_col"])
    dbq = k.dma(brow[0:1, :], b["in"]["l0_bq"][0:1, :])
    t_blkin = dbq
    t_bc1 = t_bc2 = t_bc3 = t_blkin
    bk_col, bo_col = bcols[:, 0:8], bcols[:, 8:16]

    k.on("gpsimd", lambda e: e.tensor_copy(bq_bf[0:1, :], brow), after=[t_blkin])
    t_bq = k.tok("gpsimd")
    dbv = k.dma(brow, b["in"]["l0_bv"][0:1, :], after=[t_bq])
    k.on("gpsimd", lambda e: e.tensor_copy(bv_bf[0:1, :], brow), after=[dbv])
    t_bv = k.tok("gpsimd")
    k.stage_guard[0] = t_bv

    xb_toks = []
    for c in range(NCH):
        k.on("scalar", lambda e, c=c: e.copy(xb[c][:], xr[c][:]), after=[xr_toks[c]])
        xb_toks.append(k.tok("scalar"))

    if PHASE < 1:
        return xr_toks
    st = emit_stats(k, xb, xb_toks, sq, negmean_bf=negmean_bf, std_bf=std_bf)
    t_rep = emit_replicate(k, rstd_rep[:], b["row_rstd"][0:1, :], 4, [st["rstd"]])
    if PHASE < 2:
        return xr_toks

    wq_toks = stream_wfull(k, b["in"]["l0_wq"], wfull)

    if PHASE < 3:
        return xr_toks
    t_wqs = None
    for half in range(2):
        k.bank("b7", after=[t_wqs])
        for c in range(NCH):
            k.on("tensor", lambda e, c=c, half=half: e.matmul(
                pb[7][0:1, :], b["ones_col_bf"][:], wfull[c][:, half * 512:(half + 1) * 512],
                start=(c == 0), stop=(c == NCH - 1)), after=[wq_toks[c]])
        tmm = k.tok("tensor")
        k.on("scalar", lambda e, half=half: e.activation(
            wsum_bf[0:1, half * 512:(half + 1) * 512], pb[7][0:1, :], AF.Copy,
            bias=0.0, scale=1.0), after=[tmm])
        t_wqs = k.tok("scalar")
        k.bank_done("b7", t_wqs)

    dkp = k.dma(b["wstage"][0][:, 0:256], b["in"]["l0_kp"][:],
                after=[k.stage_guard[0]])
    k.on("gpsimd", lambda e: e.tensor_copy(kp_bf, b["wstage"][0][:, 0:256]), after=[dkp])
    t_kp = k.tok("gpsimd")
    k.stage_guard[0] = t_kp

    def small_chain(dram, rhs_bf, out_bf, bias_col, wck, bias_tok, first_extra=()):
        mm_prev = None
        for c in range(NCH):
            stg = b["wstage"][c % 2]
            dtok = k.dma(stg[:], dram[c * 128:(c + 1) * 128, :],
                         after=[k.stage_guard[c % 2]], sem=f"w{c % 2}")
            k.on("gpsimd", lambda e, stg=stg, wck=wck: e.tensor_copy(wchunk[wck][:], stg[:]),
                 after=[dtok, mm_prev])
            tcast = k.tok("gpsimd")
            k.stage_guard[c % 2] = tcast
            if c == 0:
                k.bank("b5", after=list(first_extra))
            for n in range(NCH):
                k.on("tensor", lambda e, c=c, n=n, wck=wck: e.matmul(
                    pb[5][:, n * 32:(n + 1) * 32],
                    wchunk[wck][:, n * 128:(n + 1) * 128],
                    rhs_bf[:, c * 32:(c + 1) * 32],
                    start=(c == 0 and n == 0), stop=(c == NCH - 1 and n == NCH - 1)),
                    after=[tcast])
            mm_prev = k.tok("tensor")
        t_out = None
        for n in range(NCH):
            k.on("vector", lambda e, n=n: e.tensor_scalar(
                out_bf[:, n * 32:(n + 1) * 32], pb[5][:, n * 32:(n + 1) * 32],
                bias_col[:, n:n + 1], None, ALU.add), after=[mm_prev, bias_tok])
            t_out = k.tok("vector")
        k.bank_done("b5", t_out)
        return t_out

    def fill_keys_pad(t_keys):
        k.on("vector", lambda e: e.memset(keys_pad[:], 0.0))
        for n in range(NCH):
            k.on("vector", lambda e, n=n: e.tensor_copy(
                keys_pad[0:64, (2 * n) * 32:(2 * n) * 32 + 32],
                keys_bf[0:64, n * 32:(n + 1) * 32]), after=[t_keys])
            k.on("vector", lambda e, n=n: e.tensor_copy(
                keys_pad[64:128, (2 * n + 1) * 32:(2 * n + 1) * 32 + 32],
                keys_bf[64:128, n * 32:(n + 1) * 32]))
        return k.tok("vector")

    if PHASE < 4:
        return xr_toks
    t_keys = small_chain(b["in"]["l0_wk"], kp_bf, keys_bf, bk_col, 0, t_bc1,
                         first_extra=[st["bank56_free"], t_kp])
    t_keyspad = fill_keys_pad(t_keys)

    if PHASE < 5:
        return xr_toks
    mm_prev = t_keys
    for c in range(NCH):
        stg = b["wstage"][c % 2]
        dtok = k.dma(stg[:], b["in"]["l0_wv"][c * 128:(c + 1) * 128, :],
                     after=[k.stage_guard[c % 2]], sem=f"w{c % 2}")
        k.on("gpsimd", lambda e, stg=stg: e.tensor_copy(wchunk[1][:], stg[:]),
             after=[dtok, mm_prev])
        tcast = k.tok("gpsimd")
        k.stage_guard[c % 2] = tcast
        if c == 0:
            k.bank("b4", after=[t_keys, t_rep])
            k.bank("b7", after=[t_keys, t_wqs])
        for half in range(2):
            pbk = pb[4] if half == 0 else pb[7]
            k.on("tensor", lambda e, c=c, half=half, pbk=pbk: e.matmul(
                pbk[0:32, :], keys_bf[:, c * 32:(c + 1) * 32],
                wchunk[1][:, half * 512:(half + 1) * 512],
                start=(c == 0), stop=False), after=[tcast])
        mm_prev = k.tok("tensor")
    t_vals = None
    for half in range(2):
        pbk = pb[4] if half == 0 else pb[7]
        k.on("tensor", lambda e, half=half, pbk=pbk: e.matmul(
            pbk[0:32, :], b["ones_row_bf"][0:1, 0:32], bv_bf[0:1, half * 512:(half + 1) * 512],
            start=False, stop=True), after=[t_bv])
        tmm = k.tok("tensor")
        k.on("scalar", lambda e, half=half, pbk=pbk: e.activation(
            vals_sb[0:32, half * 512:(half + 1) * 512], pbk[0:32, :], AF.Copy,
            bias=0.0, scale=1.0), after=[tmm])
        t_vals = k.tok("scalar")
        k.bank_done("b4" if half == 0 else "b7", t_vals)

    if PHASE < 6:
        return xr_toks
    xr_new = list(xr_toks)
    aT_toks = [None] * 4
    q_toks = [None] * NCH
    last_scores = None
    last_q_mm = None
    for s in range(NSL):
        sl = slice(s * SL, (s + 1) * SL)
        for n in range(NCH):
            bid = n % 4
            bank = pb[bid]
            k.bank(bid, after=[wq_toks[0], xb_toks[0]])
            for c in range(NCH):
                k.on("tensor", lambda e, n=n, c=c, sl=sl, bank=bank: e.matmul(
                    bank[:], wfull[c][:, n * 128:(n + 1) * 128], xb[c][:, sl],
                    start=(c == 0), stop=False), after=[wq_toks[c], xb_toks[c]])
            k.on("tensor", lambda e, n=n, sl=sl, bank=bank: e.matmul(
                bank[:], wsum_bf[0:1, n * 128:(n + 1) * 128], negmean_bf[0:1, sl],
                start=False, stop=False), after=[t_wqs, st["negmean"]])
            k.on("tensor", lambda e, n=n, sl=sl, bank=bank: e.matmul(
                bank[:], bq_bf[0:1, n * 128:(n + 1) * 128], std_bf[0:1, sl],
                start=False, stop=True), after=[t_bq, st["std"]])
            tmm = k.tok("tensor")
            k.on("vector", lambda e, n=n, sl=sl, bank=bank: e.tensor_tensor(
                qsl[n][:], bank[:], rstd_rep[:, sl], ALU.mult),
                after=[tmm, t_rep, last_scores])
            q_toks[n] = k.tok("vector")
            k.bank_done(bid, q_toks[n])
        if s == NSL - 1:
            last_q_mm = k.tok("tensor")
        if SUB < 2:
            continue
        for ti in range(4):
            i = s * 4 + ti
            tisl = slice(ti * 128, (ti + 1) * 128)
            bid = "sc%d" % (i % 2)
            bank = pb[6 if i % 2 == 0 else 7]
            k.bank(bid, after=[st["bank56_free"], t_wqs, t_vals])
            for ch in range(NCH):
                k.on("tensor", lambda e, ch=ch, tisl=tisl, bank=bank: e.matmul(
                    bank[:, ch * 64:(ch + 1) * 64],
                    qsl[ch][:, tisl],
                    keys_pad[:, ch * 64:(ch + 1) * 64],
                    start=(ch == 0), stop=(ch == NCH - 1)),
                    after=[q_toks[ch], t_keyspad])
            tmm = k.tok("tensor")
            last_scores = tmm
            k.on("scalar", lambda e, bank=bank: e.activation(
                z[:], bank[:], AF.Copy, bias=0.0, scale=0.125), after=[tmm])
            tz = k.tok("scalar")
            k.bank_done(bid, tz)
            t_abf = emit_sparsemax(k, z, zs_a, zs_b, tau, a_bf, [tz])
            if SUB < 6:
                continue
            for j in range(4):
                k.bank("b5", after=[t_abf])
                k.on("tensor", lambda e, j=j: e.transpose(
                    pb[5][:].bitcast(BF16)[:, 0:128], a_bf[:, j * 128:(j + 1) * 128],
                    b["eye_bf"][:]))
                ttr = k.tok("tensor")
                k.on("scalar", lambda e, j=j, i=i: e.activation(
                    aT[j][:, i * 128:(i + 1) * 128], pb[5][:].bitcast(BF16)[:, 0:128],
                    AF.Copy, bias=0.0, scale=1.0), after=[ttr])
                aT_toks[j] = k.tok("scalar")
                k.bank_done("b5", aT_toks[j])
    if PHASE < 7:
        return xr_new
    wo_toks = stream_wfull(k, b["in"]["l0_wo"], wfull, after=[last_q_mm])
    vals_pad = zs_b[:].bitcast(BF16)
    o64 = z[:].bitcast(BF16)
    o64_guard = [None]
    k.on("vector", lambda e: e.memset(vals_pad, 0.0), after=aT_toks)
    t_vpad = k.tok("vector")
    for h in range(16):
        t_vpad = k.dma(vals_pad[(h % 4) * 32:(h % 4) * 32 + 32, h * 64:(h + 1) * 64],
                       vals_sb[0:32, h * 64:(h + 1) * 64], after=[t_vpad, t_vals],
                       sem="w2")
    for s in range(NSL):
        sl = slice(s * SL, (s + 1) * SL)
        oall_toks = [None] * NCH
        for kc in range(NCH):
            par = kc % 2
            bids = ("oq%da" % par, "oq%db" % par)
            banks = (pb[4 + 2 * par], pb[5 + 2 * par])
            for jj in range(2):
                k.bank(bids[jj], after=aT_toks + [t_vpad])
                j = 2 * kc + jj
                for hh in range(4):
                    h = 4 * s + hh
                    k.on("tensor", lambda e, hh=hh, h=h, j=j,
                         bank=banks[jj]: e.matmul(
                        bank[0:64, hh * 128:(hh + 1) * 128],
                        vals_pad[:, h * 64:(h + 1) * 64],
                        aT[s][:, j::16],
                        start=(hh == 0), stop=(hh == 3)))
            tmm = k.tok("tensor")
            k.on("scalar", lambda e, kc=kc, bank=banks[0]: e.activation(
                oall[kc][0:64, :], bank[0:64, :], AF.Copy, bias=0.0, scale=1.0),
                after=[tmm])
            te0 = k.tok("scalar")
            k.bank_done(bids[0], te0)
            k.on("scalar", lambda e, bank=banks[1]: e.activation(
                o64[0:64, 0:512], bank[0:64, :], AF.Copy, bias=0.0, scale=1.0),
                after=[tmm, o64_guard[0]])
            te1 = k.tok("scalar")
            k.bank_done(bids[1], te1)
            oall_toks[kc] = k.dma(oall[kc][64:128, :], o64[0:64, 0:512],
                                  after=[te0, te1], sem="w3")
            o64_guard[0] = oall_toks[kc]
        for c in range(NCH):
            bid = c % 2
            bank = pb[bid]
            k.bank(bid, after=[wo_toks[0]])
            for kc in range(NCH):
                k.on("tensor", lambda e, c=c, kc=kc, bank=bank: e.matmul(
                    bank[:], wfull[kc][:, c * 128:(c + 1) * 128], oall[kc][:],
                    start=(kc == 0), stop=(kc == NCH - 1)),
                    after=[wo_toks[kc], oall_toks[kc]])
            tmm = k.tok("tensor")
            k.on("vector", lambda e, c=c, sl=sl, bank=bank: e.scalar_tensor_tensor(
                xr[c][:, sl], bank[:], bo_col[:, c:c + 1], xr[c][:, sl],
                ALU.add, ALU.add), after=[tmm, t_bc3, xr_new[c]])
            xr_new[c] = k.tok("vector")
            k.bank_done(bid, xr_new[c])
    if DEV_TAPS:
        nc_ = k.nc
        k.barrier()
        U16 = mybir.dt.uint16
        def dump(name, objs):
            objs = objs if isinstance(objs, list) else [objs]
            aps = [(t[:] if hasattr(t, "shape") else t) for t in objs]
            wtot = sum(ap.shape[1] for ap in aps)
            pmax = max(ap.shape[0] for ap in aps)
            out = nc_.dram_tensor(name, [pmax, wtot], U16, kind="ExternalOutput").ap()
            off = 0
            for ap in aps:
                k.dma(out[0:ap.shape[0], off:off + ap.shape[1]],
                      ap.bitcast(U16), sem="out")
                off += ap.shape[1]
        dump("t_qsl", qsl)
        dump("t_keyspad", keys_pad)
        dump("t_valssb", vals_sb)
        dump("t_aT", aT)
        dump("t_oall", oall)
        k._wait("sync", [Tok("d_out", k.cnt["d_out"])])
    return xr_new




def _taps2(k, tensors):
    nc_ = k.nc
    k.barrier()
    U16 = mybir.dt.uint16
    for name, objs in tensors.items():
        objs = objs if isinstance(objs, list) else [objs]
        aps = [(t if "AP" in type(t).__name__ else t[:]) for t in objs]
        aps = [ap.bitcast(U16) for ap in aps]
        wtot = sum(ap.shape[1] for ap in aps)
        pmax = max(ap.shape[0] for ap in aps)
        out = nc_.dram_tensor("f_t_" + name, [pmax, wtot], U16,
                              kind="ExternalOutput").ap()
        off = 0
        for ap in aps:
            k.dma(out[0:ap.shape[0], off:off + ap.shape[1]], ap, sem="out")
            off += ap.shape[1]
    k._wait("sync", [Tok("d_out", k.cnt["d_out"])])


def block_l0ffn(k, xr, xr_toks, scope):
    nc, b = k.nc, k.b
    pb = b["pb"]
    sb = scope

    xb = [sb(f"f_xb{c}", (128, 2048), BF16) for c in range(NCH)]
    sq = [sb(f"f_sq{i}", (128, 512), BF16) for i in range(2)]
    negmean_bf = sb("f_negmean", (1, 2048), BF16)
    sim_bf = sb("f_sim", (128, 128), BF16)
    sqs_bf = sb("f_sqs", (128, 128), BF16)
    rsqcol = sb("f_rsqcol", (16, 1), F32)
    rr_colT = sb("f_rrcol", (128, 16), F32)
    rstd_colT = sb("f_rstdcol", (128, 16), F32)
    lfm = sb("f_lfm", (16, 512), F32)
    gmx = sb("f_gmx", (128, 16), F32)
    gmx_c = sb("f_gmxc", (128, 16), F32)
    gmx2 = sb("f_gmx2", (128, 16), F32)
    gmx2_c = sb("f_gmx2c", (128, 16), F32)
    inact = sb("f_inact", (128, 16), F32)
    inact_c = sb("f_inactc", (128, 16), F32)
    ssumc = sb("f_ssum", (128, 16), F32)
    ssum_c = sb("f_ssumc", (128, 16), F32)
    sfac = sb("f_sfac", (128, 16), F32)
    sfac_c = sb("f_sfacc", (128, 16), F32)
    rw = sb("f_rw", (128, 256), F32)
    rwT = sb("f_rwT", (16, 2048), BF16)
    rwT2 = sb("f_rwT2", (2, 2048), BF16)
    wpair = [sb(f"f_wp{i}", (128, 1024), BF16) for i in range(2)]
    qpsum_bf = sb("f_qps", (1, 128), BF16)
    rep = sb("f_rep", (128, 2048), BF16)
    tmp_pair = [sb(f"f_tp{i}", (128, 512), BF16) for i in range(2)]
    qf = sb("f_qf", (64, 2048), BF16)
    kf = sb("f_kf", (64, 2048), BF16)
    vf = sb("f_vf", (64, 2048), BF16)
    v_tm = sb("f_vtm", (128, 1024), BF16)
    p_bf = sq
    o_att = sb("f_oatt", (128, 512), F32)
    recip_rep = rep[:].bitcast(F32)[0:64, 0:512]
    opair = [sb(f"f_op{p}", (128, 1024), BF16) for p in range(8)]
    spair = [sb(f"f_sp{p}", (128, 512), BF16) for p in range(8)]
    repT = sb("f_repT", (128, 512), BF16)
    repf = rep[:].bitcast(F32)
    lgt = repf[:, 0:256]
    sup = repf[:, 256:512]
    fbm = repf[:, 512:768]
    sg_wide = repf[:, 768:1024]
    pos = sq[0][:].bitcast(F32)
    expv = sq[1][:].bitcast(F32)

    b["cmask"] = [sb(f"f_cm{j}", (128, 512), BF16) for j in range(4)]
    for j in range(4):
        t_cm = k.dma(b["cmask"][j][:], b["in"]["c_cmask"][j].bitcast(BF16))
    dsim = k.dma(b["wstage"][0][:, 0:128], b["in"]["l0f_sim_t"], after=[k.stage_guard[0]])
    k.on("gpsimd", lambda e: e.tensor_copy(sim_bf[:], b["wstage"][0][:, 0:128]),
         after=[dsim])
    t_sim = k.tok("gpsimd")
    k.stage_guard[0] = t_sim
    xb_toks = []
    for c in range(NCH):
        k.on("scalar", lambda e, c=c: e.copy(xb[c][:], xr[c][:]), after=[xr_toks[c]])
        xb_toks.append(k.tok("scalar"))
    st = emit_stats(k, xb, xb_toks, sq, negmean_bf=negmean_bf)

    rows = {"mean": b["row_mean"], "msvar": b["row_msvar"], "rstd": b["row_rstd"]}
    k.on("vector", lambda e: e.tensor_tensor(
        rows["mean"][0:1, :], rows["msvar"][0:1, :], rows["rstd"][0:1, :], ALU.mult),
        after=[st["rstd"], st["negmean"]])
    k.on("vector", lambda e: e.tensor_tensor(
        rows["mean"][0:1, :], rows["mean"][0:1, :], rows["rstd"][0:1, :], ALU.mult))
    t_t2 = k.tok("vector")
    k.on("scalar", lambda e: e.activation(
        rows["mean"][0:1, :], rows["mean"][0:1, :], AF.Sqrt, bias=0.0, scale=1024.0),
        after=[t_t2])
    t_sq = k.tok("scalar")
    k.on("vector", lambda e: e.reciprocal(rows["mean"][0:1, :], rows["mean"][0:1, :]),
         after=[t_sq])
    k.on("vector", lambda e: e.tensor_tensor(
        rows["mean"][0:1, :], rows["mean"][0:1, :], rows["rstd"][0:1, :], ALU.mult))
    t_rr = k.tok("vector")

    dg2 = k.dma(lfm[0:1, 0:16], b["in"]["l0f_gates"][0:1, :], after=[t_rr])
    k.on("scalar", lambda e: e.activation(
        lfm[0:1, 16:32], lfm[0:1, 0:16], AF.Sigmoid, bias=0.0, scale=1.0), after=[dg2])
    t_sg = k.tok("scalar")
    t_sgrow = None
    for j in range(16):
        t_sgrow = k.dma(lfm[0:1, 256 + j * 16:256 + (j + 1) * 16],
                        lfm[0:1, 16:32], after=[t_sg], sem="w3")

    tcol = None
    for i in range(16):
        k.bank("b4", after=[t_rr])
        k.on("tensor", lambda e, i=i: e.matmul(
            pb[4][0:128, 0:1], rows["mean"][0:1, i * 128:(i + 1) * 128],
            b["ones_row_f32"][0:1, 0:1], start=True, stop=True))
        k.on("tensor", lambda e, i=i: e.matmul(
            pb[4][0:128, 1:2], rows["rstd"][0:1, i * 128:(i + 1) * 128],
            b["ones_row_f32"][0:1, 0:1], start=True, stop=True), after=[st["rstd"]])
        tmm = k.tok("tensor")
        k.on("scalar", lambda e, i=i: e.activation(
            rr_colT[:, i:i + 1], pb[4][0:128, 0:1], AF.Copy, bias=0.0, scale=1.0),
            after=[tmm])
        k.on("scalar", lambda e, i=i: e.activation(
            rstd_colT[:, i:i + 1], pb[4][0:128, 1:2], AF.Copy, bias=0.0, scale=1.0))
        tcol = k.tok("scalar")
        k.bank_done("b4", tcol)

    k.bank("b4", after=[t_sgrow])
    k.on("tensor", lambda e: e.matmul(
        pb[4][0:128, 0:256], b["ones_row_f32"][0:1, :], lfm[0:1, 256:512],
        start=True, stop=True))
    tmm = k.tok("tensor")
    k.on("scalar", lambda e: e.activation(
        sg_wide, pb[4][0:128, 0:256], AF.Copy, bias=0.0, scale=1.0), after=[tmm])
    t_sgw = k.tok("scalar")
    k.bank_done("b4", t_sgw)

    k.on("scalar", lambda e: e.activation(
        sqs_bf[:], sim_bf[:], AF.Square, bias=0.0, scale=1.0), after=[t_sim])
    t_sq2 = k.tok("scalar")
    k.bank("b5", after=[st["bank56_free"], t_sq2])
    for c in range(NCH):
        k.on("tensor", lambda e, c=c: e.matmul(
            pb[5][0:1, 0:16], b["ones_col_bf"][:], sqs_bf[:, c * 16:(c + 1) * 16],
            start=(c == 0), stop=(c == NCH - 1)))
    tmm = k.tok("tensor")
    k.on("scalar", lambda e: e.activation(
        lfm[0:1, 32:48], pb[5][0:1, 0:16], AF.Sqrt, bias=0.0, scale=1.0), after=[tmm])
    t_n = k.tok("scalar")
    k.bank_done("b5", t_n)
    k.on("vector", lambda e: e.reciprocal(lfm[0:1, 32:48], lfm[0:1, 32:48]), after=[t_n])
    t_rn = k.tok("vector")
    k.bank("b4", after=[t_rn])
    k.on("tensor", lambda e: e.matmul(
        pb[4][0:16, 0:1], lfm[0:1, 32:48], b["ones_row_f32"][0:1, 0:1],
        start=True, stop=True))
    tmm = k.tok("tensor")
    k.on("scalar", lambda e: e.activation(
        rsqcol[:], pb[4][0:16, 0:1], AF.Copy, bias=0.0, scale=1.0), after=[tmm])
    t_rsq = k.tok("scalar")
    k.bank_done("b4", t_rsq)

    k.bank("b5", after=[t_n])
    for c in range(NCH):
        k.on("tensor", lambda e, c=c: e.matmul(
            pb[5][0:1, 0:16], b["ones_col_bf"][:], sim_bf[:, c * 16:(c + 1) * 16],
            start=(c == 0), stop=(c == NCH - 1)))
    tmm = k.tok("tensor")
    k.on("scalar", lambda e: e.activation(
        lfm[0:1, 48:64], pb[5][0:1, 0:16], AF.Copy, bias=0.0, scale=1.0), after=[tmm])
    t_ss = k.tok("scalar")
    k.bank_done("b5", t_ss)
    k.on("vector", lambda e: e.tensor_copy(rwT[0:1, 0:16], lfm[0:1, 48:64]),
         after=[t_ss])
    t_ssbf = k.tok("vector")

    t_lgt = None
    for s in range(NSL):
        sl = slice(s * SL, (s + 1) * SL)
        k.bank("b7", after=[t_ssbf, st["negmean"]] + ([] if s else [st["bank56_free"]]))
        for c in range(NCH):
            k.on("tensor", lambda e, c=c, sl=sl: e.matmul(
                pb[7][0:16, :], sim_bf[:, c * 16:(c + 1) * 16], xb[c][:, sl],
                start=(c == 0), stop=False), after=[xb_toks[c], t_sim])
        k.on("tensor", lambda e, sl=sl: e.matmul(
            pb[7][0:16, :], rwT[0:1, 0:16], negmean_bf[0:1, sl],
            start=False, stop=True))
        tmm = k.tok("tensor")
        k.on("vector", lambda e: e.tensor_scalar(
            lfm[:, :], pb[7][0:16, :], rsqcol[:], None, ALU.mult),
            after=[tmm, t_rsq, k.b.get("lfm_guard")])
        t_lfm = k.tok("vector")
        k.bank_done("b7", t_lfm)
        for ti in range(4):
            i = s * 4 + ti
            k.bank("b6", after=[t_lfm] + ([st["bank56_free"]] if i == 0 else []))
            k.on("tensor", lambda e, ti=ti: e.transpose(
                pb[6][0:128, 0:16], lfm[:, ti * 128:(ti + 1) * 128],
                b["eye_f32"][0:16, 0:16]))
            ttr = k.tok("tensor")
            k.on("scalar", lambda e, i=i: e.activation(
                lgt[:, i * 16:(i + 1) * 16], pb[6][0:128, 0:16], AF.Copy,
                bias=0.0, scale=1.0), after=[ttr])
            t_lgt = k.tok("scalar")
            k.bank_done("b6", t_lgt)
            k.b["lfm_guard"] = t_lgt

    g3 = lambda ap: ap.rearrange("p (g s) -> p g s", g=16)
    bc = lambda t: t[:].unsqueeze(2).broadcast_to([128, 16, 16])
    k.on("vector", lambda e: e.tensor_tensor(
        g3(lgt), g3(lgt), bc(rr_colT), ALU.mult), after=[t_lgt, tcol])
    k.on("vector", lambda e: e.tensor_tensor(
        lgt, lgt, sg_wide, ALU.subtract), after=[t_sgw])
    t_lgtfin = k.tok("vector")
    k.on("vector", lambda e: e.reduce_max(gmx[:], g3(lgt), axis=AX.X))
    tv = k.tok("vector")
    k.on("gpsimd", lambda e: e.tensor_copy(gmx_c[:], gmx[:]), after=[tv])
    tg = k.tok("gpsimd")
    k.on("scalar", lambda e: e.activation(
        expv, lgt, AF.Exp, bias=0.0, scale=1.0), after=[t_lgtfin])
    t_exp = k.tok("scalar")
    k.on("vector", lambda e: e.tensor_tensor(
        g3(sup), g3(lgt), bc(gmx_c), ALU.is_ge), after=[tg])
    k.on("vector", lambda e: e.scalar_tensor_tensor(
        sup, sup, -1e30, lgt, ALU.mult, ALU.add))
    k.on("vector", lambda e: e.reduce_max(gmx2[:], g3(sup), axis=AX.X))
    k.on("vector", lambda e: e.tensor_scalar(pos, lgt, 0.0, None, ALU.is_gt))
    k.on("vector", lambda e: e.tensor_scalar(
        inact[:], gmx_c[:], 0.0, None, ALU.is_le))
    tv = k.tok("vector")
    k.on("gpsimd", lambda e: e.tensor_copy(gmx2_c[:], gmx2[:]), after=[tv])
    k.on("gpsimd", lambda e: e.tensor_copy(inact_c[:], inact[:]))
    tg = k.tok("gpsimd")
    k.on("vector", lambda e: e.tensor_tensor(
        g3(fbm), g3(lgt), bc(gmx2_c), ALU.is_ge), after=[tg])
    k.on("vector", lambda e: e.tensor_tensor(fbm, fbm, pos, ALU.subtract))
    k.on("vector", lambda e: e.tensor_tensor(
        g3(fbm), g3(fbm), bc(inact_c), ALU.mult))
    k.on("vector", lambda e: e.tensor_tensor(fbm, fbm, pos, ALU.add))
    k.on("vector", lambda e: e.tensor_scalar_max(expv, expv, 1.0), after=[t_exp])
    k.on("vector", lambda e: e.tensor_tensor(fbm, fbm, expv, ALU.mult))
    k.on("vector", lambda e: e.reduce_sum(ssumc[:], g3(fbm), axis=AX.X))
    tv = k.tok("vector")
    k.on("gpsimd", lambda e: e.tensor_copy(ssum_c[:], ssumc[:]), after=[tv])
    tg = k.tok("gpsimd")
    k.on("vector", lambda e: e.reciprocal(sfac[:], ssum_c[:]), after=[tg])
    tv = k.tok("vector")
    k.on("gpsimd", lambda e: e.tensor_copy(sfac_c[:], sfac[:]), after=[tv])
    tg = k.tok("gpsimd")
    k.on("vector", lambda e: e.tensor_tensor(
        g3(rw[:]), g3(fbm), bc(sfac_c), ALU.mult), after=[tg])
    k.on("vector", lambda e: e.tensor_tensor(
        g3(rw[:]), g3(rw[:]), bc(rstd_colT), ALU.mult))
    t_rwfin = k.tok("vector")

    t_rwT = None
    for i in range(16):
        k.bank("b6", after=[t_rwfin, t_ssbf])
        k.on("tensor", lambda e, i=i: e.transpose(
            pb[6][0:16, 0:128], rw[:, i * 16:(i + 1) * 16], b["eye_f32"][:]))
        ttr = k.tok("tensor")
        k.on("scalar", lambda e, i=i: e.activation(
            rwT[0:16, i * 128:(i + 1) * 128], pb[6][0:16, 0:128], AF.Copy,
            bias=0.0, scale=1.0), after=[ttr])
        t_rwT = k.tok("scalar")
        k.bank_done("b6", t_rwT)
    k.b["t_rwT_tok"] = t_rwT

    if FFN_PHASE < 1:
        if DEV_TAPS2:
            _taps2(k, dict(xb0=xb[0], negmean=negmean_bf, rows=[
                rr_colT, rstd_colT, rsqcol[:]], rw=rw, rwT=rwT, lgt=lgt,
                fbm=fbm, sgw=sg_wide))
        return xr_toks

    def expert_proj(wkey, dst, facc_banks):
        nonlocal_toks = k.b.setdefault("ep_toks", [])
        out_toks = [None] * NSL
        for p in range(8):
            stg = b["wstage"][p % 2]
            dtok = k.dma(stg[:], b["in"][wkey][p],
                         after=[k.stage_guard[p % 2]], sem=f"w{p % 2}")
            k.on("gpsimd", lambda e, stg=stg, p=p: e.tensor_copy(wpair[p % 2][:], stg[:]),
                 after=[dtok, nonlocal_toks[-2] if len(nonlocal_toks) >= 2 else None])
            tcast = k.tok("gpsimd")
            k.stage_guard[p % 2] = tcast
            k.bank("b4", after=[tcast])
            for c in range(NCH):
                k.on("tensor", lambda e, c=c, p=p: e.matmul(
                    pb[4][0:1, 0:128], b["ones_col_bf"][:],
                    wpair[p % 2][:, c * 128:(c + 1) * 128],
                    start=(c == 0), stop=(c == NCH - 1)))
            tmm = k.tok("tensor")
            k.on("scalar", lambda e: e.activation(
                qpsum_bf[0:1, :], pb[4][0:1, 0:128], AF.Copy, bias=0.0, scale=1.0),
                after=[tmm, k.b.get("qps_guard")])
            t_qps = k.tok("scalar")
            k.bank_done("b4", t_qps)
            t_rw2 = k.dma(rwT2[0:2, :], rwT[2 * p:2 * p + 2, :],
                          after=[t_rwT, k.b.get("rw2_guard")], sem="w3")
            t_rep = None
            for s in range(NSL):
                sl = slice(s * SL, (s + 1) * SL)
                k.bank("b5", after=[t_rwT, t_rw2])
                k.on("tensor", lambda e, sl=sl: e.matmul(
                    pb[5][0:128, :], b["pair2_bf"][:], rwT2[0:2, sl],
                    start=True, stop=True))
                tmm = k.tok("tensor")
                k.on("scalar", lambda e, sl=sl: e.activation(
                    rep[:, sl], pb[5][0:128, :], AF.Copy, bias=0.0, scale=1.0),
                    after=[tmm, k.b.get("rep_guard%d" % s)])
                t_rep = k.tok("scalar")
                k.bank_done("b5", t_rep)
            k.b["rw2_guard"] = t_rep
            for s in range(NSL):
                sl = slice(s * SL, (s + 1) * SL)
                bid = "ep%d" % (s % 2)
                bank = pb[s % 2]
                k.bank(bid, after=[t_qps, t_rep])
                for c in range(NCH):
                    k.on("tensor", lambda e, c=c, p=p, sl=sl, bank=bank: e.matmul(
                        bank[:], wpair[p % 2][:, c * 128:(c + 1) * 128], xb[c][:, sl],
                        start=(c == 0), stop=False))
                k.on("tensor", lambda e, sl=sl, bank=bank: e.matmul(
                    bank[:], qpsum_bf[0:1, :], negmean_bf[0:1, sl],
                    start=False, stop=True))
                tmm = k.tok("tensor")
                tp = tmp_pair[s % 2]
                k.on("vector", lambda e, sl=sl, bank=bank, tp=tp: e.tensor_tensor(
                    tp[:], bank[:], rep[:, sl], ALU.mult),
                    after=[tmm, k.b.get("tp_guard%d" % (s % 2))])
                t_tp = k.tok("vector")
                k.bank_done(bid, t_tp)
                k.b["rep_guard%d" % s] = t_tp
                fb = facc_banks[s]
                k.bank("fa%d" % fb, after=[t_tp, k.b.get("t_rwT_tok")])
                k.on("tensor", lambda e, p=p, fb=fb, tp=tp: e.matmul(
                    pb[fb][0:64, :], b["fold64_bf"][:], tp[:],
                    start=(p == 0), stop=(p == 7)))
                tmm2 = k.tok("tensor")
                k.b["tp_guard%d" % (s % 2)] = tmm2
                if p == 7:
                    k.on("scalar", lambda e, s=s, sl=sl, fb=fb: e.activation(
                        dst[0:64, sl], pb[fb][0:64, :], AF.Copy, bias=0.0, scale=1.0),
                        after=[tmm2])
                    out_toks[s] = k.tok("scalar")
                    k.bank_done("fa%d" % fb, out_toks[s])
            nonlocal_toks.append(k.tok("tensor"))
            k.b["qps_guard"] = nonlocal_toks[-1]
        return out_toks

    tq = expert_proj("l0f_qp", qf, [2, 3, 6, 7])
    tk = expert_proj("l0f_kp", kf, [2, 3, 6, 7])
    tv = expert_proj("l0f_vp", vf, [2, 3, 6, 7])

    t_vtm = None
    for i in range(16):
        k.bank("b4", after=[tv[i // 4]])
        k.on("tensor", lambda e, i=i: e.transpose(
            pb[4][:].bitcast(BF16)[:, 0:64], vf[0:64, i * 128:(i + 1) * 128],
            b["eye_bf"][0:64, 0:64]))
        ttr = k.tok("tensor")
        k.on("scalar", lambda e, i=i: e.activation(
            v_tm[:, i * 64:(i + 1) * 64], pb[4][:].bitcast(BF16)[:, 0:64],
            AF.Copy, bias=0.0, scale=1.0), after=[ttr])
        t_vtm = k.tok("scalar")
        k.bank_done("b4", t_vtm)

    op_toks = []
    for p in range(8):
        stg = b["wstage"][p % 2]
        dtok = k.dma(stg[:], b["in"]["l0f_op"][p], after=[k.stage_guard[p % 2]],
                     sem=f"w{p % 2}")
        k.on("gpsimd", lambda e, stg=stg, p=p: e.tensor_copy(opair[p][:], stg[:]),
             after=[dtok])
        k.stage_guard[p % 2] = k.tok("gpsimd")
        op_toks.append(k.stage_guard[p % 2])

    xr_new = list(xr_toks)
    for qs in range(NSL):
        sl = slice(qs * SL, (qs + 1) * SL)
        nkt = 4 * (qs + 1)
        t_pb = [None, None]
        t_sum = None
        t_oacc = None
        for kt in range(nkt):
            bid = "at%d" % (kt % 2)
            bank = pb[kt % 2]
            k.bank(bid, after=[tq[qs], tk[kt // 4], t_vtm])
            k.on("tensor", lambda e, kt=kt, sl=sl, bank=bank: e.matmul(
                bank[:], kf[0:64, kt * 128:(kt + 1) * 128], qf[0:64, sl],
                start=True, stop=True))
            tmm = k.tok("tensor")
            pbf = p_bf[kt % 2]
            k.on("scalar", lambda e, bank=bank, pbf=pbf: e.activation(
                pbf[:], bank[:], AF.Exp, bias=0.0, scale=0.125),
                after=[tmm, t_pb[kt % 2]])
            t_p = k.tok("scalar")
            k.bank_done(bid, t_p)
            if kt >= 4 * qs:
                j = kt - 4 * qs
                k.on("vector", lambda e, pbf=pbf, j=j: e.tensor_tensor(
                    pbf[:], pbf[:], b["cmask"][j][:], ALU.mult), after=[t_p, t_cm])
                t_p = k.tok("vector")
            k.bank("b4", after=[t_p] + ([t_vtm] if kt == 0 else []))
            k.on("tensor", lambda e, kt=kt, pbf=pbf: e.matmul(
                pb[4][0:1, :], b["ones_col_bf"][:], pbf[:],
                start=(kt == 0), stop=(kt == nkt - 1)))
            k.bank("b5", after=[t_p])
            k.on("tensor", lambda e, kt=kt, pbf=pbf: e.matmul(
                pb[5][0:64, :], v_tm[:, kt * 64:(kt + 1) * 64], pbf[:],
                start=(kt == 0), stop=(kt == nkt - 1)))
            t_pb[kt % 2] = k.tok("tensor")
        t_mm_all = k.tok("tensor")
        k.on("vector", lambda e, sl=sl: e.reciprocal(
            b["row_msvar"][0:1, sl], pb[4][0:1, :]), after=[t_mm_all])
        t_rcp = k.tok("vector")
        k.bank_done("b4", t_rcp)
        k.bank("b6", after=[t_rcp])
        k.on("tensor", lambda e, sl=sl: e.matmul(
            pb[6][0:64, :], b["ones_row_f32"][0:1, 0:64], b["row_msvar"][0:1, sl],
            start=True, stop=True))
        tmm = k.tok("tensor")
        k.on("scalar", lambda e: e.activation(
            recip_rep[:], pb[6][0:64, :], AF.Copy, bias=0.0, scale=1.0), after=[tmm])
        t_rr2 = k.tok("scalar")
        k.bank_done("b6", t_rr2)
        k.on("vector", lambda e: e.tensor_tensor(
            o_att[0:64, :], pb[5][0:64, :], recip_rep[:], ALU.mult),
            after=[t_rr2] + k.b.get("oatt_guard", []))
        t_oatt = k.tok("vector")
        k.bank_done("b5", t_oatt)
        t_oatt = k.dma(o_att[64:128, :], o_att[0:64, :], after=[t_oatt], sem="w2")
        sp_toks = [None] * 8
        for p in range(8):
            t_rw2 = k.dma(rwT2[0:2, :], rwT[2 * p:2 * p + 2, :],
                          after=[t_rwT, k.b.get("rw2_guard")], sem="w3")
            k.bank("b6", after=[t_rwT, t_oatt, t_rw2])
            k.on("tensor", lambda e, sl=sl: e.matmul(
                pb[6][0:128, :], b["pair2_bf"][:], rwT2[0:2, sl],
                start=True, stop=True))
            tmm = k.tok("tensor")
            k.on("scalar", lambda e: e.activation(
                repT[:], pb[6][0:128, :], AF.Copy, bias=0.0, scale=1.0),
                after=[tmm, k.b.get("repT_guard")])
            t_repT = k.tok("scalar")
            k.bank_done("b6", t_repT)
            k.b["rw2_guard"] = t_repT
            k.on("vector", lambda e, p=p: e.tensor_tensor(
                spair[p][0:64, :], o_att[0:64, :], repT[0:64, :], ALU.mult),
                after=[t_repT, t_oatt, k.b.get("sp_guard")])
            k.on("vector", lambda e, p=p: e.tensor_tensor(
                spair[p][64:128, :], o_att[64:128, :], repT[64:128, :], ALU.mult))
            sp_toks[p] = k.tok("vector")
            k.b["repT_guard"] = sp_toks[p]
        for c in range(NCH):
            bid = "oo%d" % (c % 2)
            bank = pb[2 + (c % 2)]
            k.bank(bid, after=sp_toks + [op_toks[-1]])
            for p in range(8):
                k.on("tensor", lambda e, c=c, p=p, bank=bank: e.matmul(
                    bank[:], opair[p][:, c * 128:(c + 1) * 128], spair[p][:],
                    start=(p == 0), stop=(p == 7)), after=[op_toks[p]])
            tmm = k.tok("tensor")
            k.b["sp_guard"] = tmm
            k.on("vector", lambda e, c=c, sl=sl, bank=bank: e.tensor_tensor(
                xr[c][:, sl], bank[:], xr[c][:, sl], ALU.add),
                after=[tmm, xr_new[c]])
            xr_new[c] = k.tok("vector")
            k.bank_done(bid, xr_new[c])
        k.b["oatt_guard"] = sp_toks
    if DEV_TAPS2:
        _taps2(k, dict(xb0=xb[0], negmean=negmean_bf, rows=[
            rr_colT, rstd_colT, rsqcol[:]], rw=rw, rwT=rwT, qf=qf, kf=kf,
            vf=vf, vtm=v_tm, oatt=o_att, sp0=spair[0]))
    return xr_new


def block_l1attn(k, xr, xr_toks, scope):
    nc, b = k.nc, k.b
    pb = b["pb"]
    sb = scope
    LN16 = 2.772588722239781

    xhat = [sb(f"t2_xh{c}", (128, 2048), BF16) for c in range(NCH)]
    sub1 = contextlib.ExitStack()
    ss1 = lambda n, s, d: sub1.enter_context(nc.sbuf_tensor(n, list(s), d))
    sq = [ss1(f"t2_sq{i}", (128, 512), BF16) for i in range(2)]
    negm_rep = ss1("t2_negmrep", (128, 2048), BF16)
    rstd_rep = ss1("t2_rstdrep", (128, 2048), BF16)
    scratch = ss1("t2_scr", (128, 512), F32)
    xb_toks = []
    for c in range(NCH):
        k.on("scalar", lambda e, c=c: e.copy(xhat[c][:], xr[c][:]),
             after=[xr_toks[c]])
        xb_toks.append(k.tok("scalar"))
    st = emit_stats(k, xhat, xb_toks, sq)

    t_nm = emit_replicate(k, negm_rep, b["row_mean"], 4,
                          [st["rows"]])
    t_rs = emit_replicate(k, rstd_rep, b["row_rstd"], 5, [st["rstd"]])
    xh_tok = [None] * NCH
    for c in range(NCH):
        for s in range(NSL):
            sl = slice(s * SL, (s + 1) * SL)
            k.on("vector", lambda e, c=c, sl=sl: e.tensor_tensor(
                scratch[:], xr[c][:, sl], negm_rep[:, sl], ALU.subtract),
                after=[t_nm, t_rs, st["rows"]])
            k.on("vector", lambda e, c=c, sl=sl: e.tensor_tensor(
                xhat[c][:, sl], scratch[:], rstd_rep[:, sl], ALU.mult))
            xh_tok[c] = k.tok("vector")
    sub1.close()

    v_tm = [sb(f"t2_vtm{i}", (128, 1024), BF16) for i in range(16)]
    qfm = sb("t2_qfm", (128, 2048), BF16)
    kfm = sb("t2_kfm", (128, 2048), BF16)
    wqk_bf = [sb(f"t2_wqk{j}", (128, 1024), BF16) for j in range(2)]
    fmq_bf = sb("t2_fmqw", (128, 128), BF16)
    fmk_bf = sb("t2_fmkw", (128, 128), BF16)
    qf_pos = sb("t2_qfp", (128, 128), BF16)
    qf_neg = sb("t2_qfn", (128, 128), BF16)
    kf_pos = sb("t2_kfp", (128, 128), BF16)
    kf_neg = sb("t2_kfn", (128, 128), BF16)
    ktm = sb("t2_ktm", (128, 256), BF16)
    p_bf = sb("t2_pbf", (128, 128), BF16)
    sq_bf = sb("t2_sqo", (128, 128), BF16)
    sq_bf2 = sq_bf[:].bitcast(BF16)
    S_pos = sb("t2_Sp", (128, 128), F32)
    S_neg = sb("t2_Sn", (128, 128), F32)
    Sb_pos = sb("t2_Sbp", (128, 128), BF16)
    Sb_neg = sb("t2_Sbn", (128, 128), BF16)
    bqp = sb("t2_bqp", (128, 1), F32)
    bqn = sb("t2_bqn", (128, 1), F32)
    bkp = sb("t2_bkp", (128, 1), F32)
    bkn = sb("t2_bkn", (128, 1), F32)
    rmsw_col = sb("t2_rmsw", (128, 1), F32)
    fmkb_row = sb("t2_fkbr", (1, 128), F32)
    fmkb_row_bf = sb("t2_fkbrb", (1, 128), BF16)
    rms_row = sb("t2_rmsrow", (1, 128), F32)

    d1 = k.dma(bqp[:], b["in"]["l1_fmqb_col"])
    d2 = k.dma(bkp[:], b["in"]["l1_fmkb_col"])
    d3 = k.dma(fmkb_row[:], b["in"]["l1_fmkb_row"])
    d4 = k.dma(rmsw_col[:], b["in"]["l1_rmsw_col"])
    dq = k.dma(b["wstage"][0][:, 0:128], b["in"]["l1_fmqwT"],
               after=[k.stage_guard[0]])
    dk_ = k.dma(b["wstage"][0][:, 128:256], b["in"]["l1_fmkwT"])
    k.on("gpsimd", lambda e: e.tensor_copy(fmq_bf[:], b["wstage"][0][:, 0:128]),
         after=[dq, dk_])
    k.on("gpsimd", lambda e: e.tensor_copy(fmk_bf[:], b["wstage"][0][:, 128:256]))
    t_fmw = k.tok("gpsimd")
    k.stage_guard[0] = t_fmw
    k.on("vector", lambda e: e.tensor_scalar(
        bqn[:], bqp[:], -1.0, -LN16, ALU.mult, ALU.add), after=[d1])
    k.on("vector", lambda e: e.tensor_scalar(
        bqp[:], bqp[:], -LN16, None, ALU.add))
    k.on("vector", lambda e: e.tensor_scalar(
        bkn[:], bkp[:], -1.0, None, ALU.mult), after=[d2])
    k.on("vector", lambda e: e.tensor_copy(fmkb_row_bf[:], fmkb_row[:]),
         after=[d3])
    t_bias = k.tok("vector")


    sub2 = contextlib.ExitStack()
    wv_bf = [sub2.enter_context(nc.sbuf_tensor(f"t2_wv{c}", [128, 1024], BF16))
             for c in range(8)]
    t_wv = [None] * 8
    for c in range(8):
        stg = b["wstage"][c % 2]
        dtok = k.dma(stg[:], b["in"]["l1_wvT"][c * 128:(c + 1) * 128, :],
                     after=[k.stage_guard[c % 2]], sem=f"w{c % 2}")
        k.on("gpsimd", lambda e, stg=stg, c=c: e.tensor_copy(wv_bf[c][:], stg[:]),
             after=[dtok])
        t_wv[c] = k.tok("gpsimd")
        k.stage_guard[c % 2] = t_wv[c]
    vtm_tok = [None] * 16
    for i in range(16):
        for half in range(2):
            bid = "b%d" % (half)
            bank = pb[half]
            k.bank(bid, after=[t_wv[7], st["bank56_free"]])
            for c in range(NCH):
                k.on("tensor", lambda e, c=c, i=i, half=half, bank=bank: e.matmul(
                    bank[:], xhat[c][:, i * 128:(i + 1) * 128],
                    wv_bf[c][:, half * 512:(half + 1) * 512],
                    start=(c == 0), stop=(c == NCH - 1)),
                    after=[xh_tok[c]])
            tmm = k.tok("tensor")
            k.on("scalar", lambda e, i=i, half=half, bank=bank: e.activation(
                v_tm[i][:, half * 512:(half + 1) * 512], bank[:], AF.Copy,
                bias=0.0, scale=1.0), after=[tmm])
            vtm_tok[i] = k.tok("scalar")
            k.bank_done(bid, vtm_tok[i])
    sub2.close()
    o_n = [sb(f"t2_on{h}", (128, 2048), BF16) for h in range(7)] + [xhat[7]]

    on_tok = [None] * 8
    for h in range(8):
        dqt = dkt = None
        for d in range(8):
            dqt = k.dma(b["wstage"][0][:, d * 128:(d + 1) * 128],
                        b["in"]["l1_wqT"][d * 128:(d + 1) * 128,
                                          h * 128:(h + 1) * 128],
                        after=[k.stage_guard[0]], sem="w0")
            dkt = k.dma(b["wstage"][1][:, d * 128:(d + 1) * 128],
                        b["in"]["l1_wkT"][d * 128:(d + 1) * 128,
                                          h * 128:(h + 1) * 128],
                        after=[k.stage_guard[1]], sem="w1")
        k.on("gpsimd", lambda e: e.tensor_copy(wqk_bf[0][:], b["wstage"][0][:]),
             after=[dqt, k.b.get("l1_projq_done")])
        k.stage_guard[0] = k.tok("gpsimd")
        k.on("gpsimd", lambda e: e.tensor_copy(wqk_bf[1][:], b["wstage"][1][:]),
             after=[dkt, k.b.get("l1_projk_done")])
        k.stage_guard[1] = k.tok("gpsimd")
        t_wq, t_wk = k.stage_guard[0], k.stage_guard[1]
        t_qfm = t_kfm = None
        for s in range(NSL):
            sl = slice(s * SL, (s + 1) * SL)
            k.bank("b0", after=[t_wq])
            for c in range(NCH):
                k.on("tensor", lambda e, c=c, sl=sl: e.matmul(
                    pb[0][:], wqk_bf[0][:, c * 128:(c + 1) * 128],
                    xhat[c][:, sl], start=(c == 0), stop=(c == NCH - 1)),
                    after=[xh_tok[c]])
            tmm = k.tok("tensor")
            if s == NSL - 1:
                k.b["l1_projq_done"] = tmm
            k.on("scalar", lambda e, sl=sl: e.activation(
                qfm[:, sl], pb[0][:], AF.Copy, bias=0.0, scale=1.0), after=[tmm])
            t_qfm = k.tok("scalar")
            k.bank_done("b0", t_qfm)
            k.bank("b1", after=[t_wk])
            for c in range(NCH):
                k.on("tensor", lambda e, c=c, sl=sl: e.matmul(
                    pb[1][:], wqk_bf[1][:, c * 128:(c + 1) * 128],
                    xhat[c][:, sl], start=(c == 0), stop=(c == NCH - 1)))
            tmm = k.tok("tensor")
            if s == NSL - 1:
                k.b["l1_projk_done"] = tmm
            k.on("scalar", lambda e, sl=sl: e.activation(
                kfm[:, sl], pb[1][:], AF.Copy, bias=0.0, scale=1.0), after=[tmm])
            t_kfm = k.tok("scalar")
            k.bank_done("b1", t_kfm)

        for i in range(16):
            ch = slice(i * 128, (i + 1) * 128)
            hs = slice(h * 128, (h + 1) * 128)
            k.bank("b2", after=[t_qfm, t_fmw, k.b.get("l1_qf_used")])
            k.on("tensor", lambda e, ch=ch: e.matmul(
                pb[2][0:128, 0:128], fmq_bf[:], qfm[:, ch],
                start=True, stop=True))
            tmm = k.tok("tensor")
            k.on("scalar", lambda e: e.activation(
                qf_pos[:], pb[2][0:128, 0:128], AF.Exp, bias=bqp[:, 0:1],
                scale=1.0), after=[tmm, t_bias])
            k.on("scalar", lambda e: e.activation(
                qf_neg[:], pb[2][0:128, 0:128], AF.Exp, bias=bqn[:, 0:1],
                scale=-1.0))
            t_qf = k.tok("scalar")
            k.bank_done("b2", t_qf)
            k.bank("b3", after=[t_kfm, k.b.get("l1_kf_used")])
            k.on("tensor", lambda e, ch=ch: e.matmul(
                pb[3][0:128, 0:128], fmk_bf[:], kfm[:, ch],
                start=True, stop=True))
            tmm = k.tok("tensor")
            k.on("scalar", lambda e: e.activation(
                kf_pos[:], pb[3][0:128, 0:128], AF.Exp, bias=bkp[:, 0:1],
                scale=1.0), after=[tmm])
            k.on("scalar", lambda e: e.activation(
                kf_neg[:], pb[3][0:128, 0:128], AF.Exp, bias=bkn[:, 0:1],
                scale=-1.0))
            t_kf = k.tok("scalar")
            k.bank_done("b3", t_kf)
            k.bank("b4", after=[t_kfm, k.b.get("l1_ktm_used")])
            k.on("tensor", lambda e, ch=ch: e.matmul(
                pb[4][0:128, 0:128], kfm[:, ch], fmk_bf[:],
                start=True, stop=False))
            k.on("tensor", lambda e: e.matmul(
                pb[4][0:128, 0:128], b["ones_row_bf"][0:1, 0:128],
                fmkb_row_bf[0:1, :], start=False, stop=True), after=[t_bias])
            tmm = k.tok("tensor")
            k.on("scalar", lambda e: e.activation(
                ktm[:, 0:128], pb[4][0:128, 0:128], AF.Exp, bias=0.0,
                scale=1.0), after=[tmm])
            k.on("scalar", lambda e: e.activation(
                ktm[:, 128:256], pb[4][0:128, 0:128], AF.Exp, bias=0.0,
                scale=-1.0))
            t_ktm = k.tok("scalar")
            k.bank_done("b4", t_ktm)
            k.bank("b5", after=[t_qf, t_kf, st["bank56_free"],
                                k.b.get("l1_p_used")])
            k.on("tensor", lambda e: e.matmul(
                pb[5][0:128, 0:128], kf_pos[:], qf_pos[:],
                start=True, stop=False))
            k.on("tensor", lambda e: e.matmul(
                pb[5][0:128, 0:128], kf_neg[:], qf_neg[:],
                start=False, stop=True))
            tmm = k.tok("tensor")
            k.on("vector", lambda e: e.tensor_tensor(
                p_bf[:], pb[5][0:128, 0:128], b["trimask"][:], ALU.mult),
                after=[tmm])
            t_p = k.tok("vector")
            k.bank_done("b5", t_p)
            k.bank("b6", after=[t_p, vtm_tok[i], st["bank56_free"]])
            k.on("tensor", lambda e, i=i, hs=hs: e.matmul(
                pb[6][0:128, 0:128], v_tm[i][:, hs], p_bf[:],
                start=True, stop=(i == 0)), after=[t_p])
            if i > 0:
                k.on("tensor", lambda e: e.matmul(
                    pb[6][0:128, 0:128], Sb_pos[:], qf_pos[:],
                    start=False, stop=False), after=[k.b.get("l1_Sb_tok")])
                k.on("tensor", lambda e: e.matmul(
                    pb[6][0:128, 0:128], Sb_neg[:], qf_neg[:],
                    start=False, stop=True))
            t_omm = k.tok("tensor")
            k.b["l1_qf_used"] = t_omm
            k.b["l1_p_used"] = t_omm
            k.bank("b7", after=[t_ktm, vtm_tok[i], k.b.get("l1_Sadd")])
            k.on("tensor", lambda e, i=i, hs=hs: e.matmul(
                pb[7][0:128, 0:128], ktm[:, 0:128], v_tm[i][:, hs],
                start=True, stop=True))
            k.on("tensor", lambda e, i=i, hs=hs: e.matmul(
                pb[7][0:128, 128:256], ktm[:, 128:256], v_tm[i][:, hs],
                start=True, stop=True))
            t_dmm = k.tok("tensor")
            k.b["l1_ktm_used"] = t_dmm
            if i == 0:
                k.on("vector", lambda e: e.tensor_copy(
                    S_pos[:], pb[7][0:128, 0:128]), after=[t_dmm])
                k.on("vector", lambda e: e.tensor_copy(
                    S_neg[:], pb[7][0:128, 128:256]))
            else:
                k.on("vector", lambda e: e.tensor_tensor(
                    S_pos[:], S_pos[:], pb[7][0:128, 0:128], ALU.add),
                    after=[t_dmm])
                k.on("vector", lambda e: e.tensor_tensor(
                    S_neg[:], S_neg[:], pb[7][0:128, 128:256], ALU.add))
            t_Sadd = k.tok("vector")
            k.bank_done("b7", t_Sadd)
            k.b["l1_Sadd"] = t_Sadd
            if i < 15:
                k.on("gpsimd", lambda e: e.tensor_copy(Sb_pos[:], S_pos[:]),
                     after=[t_Sadd, t_omm])
                k.on("gpsimd", lambda e: e.tensor_copy(Sb_neg[:], S_neg[:]))
                k.b["l1_Sb_tok"] = k.tok("gpsimd")
            k.on("scalar", lambda e: e.activation(
                sq_bf[:], pb[6][0:128, 0:128], AF.Square, bias=0.0, scale=1.0),
                after=[t_omm])
            t_sq = k.tok("scalar")
            k.bank("b2", after=[t_sq])
            k.on("tensor", lambda e: e.matmul(
                pb[2][0:1, 0:128], b["ones_col_bf"][:], sq_bf[:],
                start=True, stop=True))
            tmm = k.tok("tensor")
            k.on("scalar", lambda e: e.activation(
                rms_row[0:1, :], pb[2][0:1, 0:128], AF.Sqrt,
                bias=b["eps_col"][0:1, 0:1], scale=1.0 / 128.0), after=[tmm])
            t_rms = k.tok("scalar")
            k.on("vector", lambda e: e.reciprocal(rms_row[0:1, :],
                                                  rms_row[0:1, :]),
                 after=[t_rms])
            t_rcp = k.tok("vector")
            k.on("tensor", lambda e: e.matmul(
                pb[2][0:128, 128:256], b["ones_row_f32"][0:1, :],
                rms_row[0:1, :], start=True, stop=True), after=[t_rcp])
            tmm = k.tok("tensor")
            k.on("scalar", lambda e: e.activation(
                sq_bf2[:], pb[2][0:128, 128:256], AF.Copy, bias=0.0,
                scale=1.0), after=[tmm])
            t_rsqs = k.tok("scalar")
            k.on("vector", lambda e, h=h, ch=ch: e.tensor_tensor(
                o_n[h][:, ch], pb[6][0:128, 0:128], sq_bf2[:],
                ALU.mult), after=[t_rsqs])
            on_tok[h] = k.tok("vector")
            k.bank_done("b2", on_tok[h])
            k.bank_done("b6", on_tok[h])

    wo_bf = [xhat[c // 2][:, (c % 2) * 1024:(c % 2 + 1) * 1024]
             for c in range(8)]
    t_wo = [None] * 8
    for hc in range(8):
        stg = b["wstage"][hc % 2]
        dtok = k.dma(stg[:], b["in"]["l1_woT"][hc * 128:(hc + 1) * 128, :],
                     after=[k.stage_guard[hc % 2]], sem=f"w{hc % 2}")
        k.on("vector", lambda e, stg=stg, hc=hc: e.tensor_scalar(
            wo_bf[hc], stg[:], rmsw_col[:, 0:1], None, ALU.mult),
            after=[dtok, d4, k.b.get("l1_projq_done"), k.b.get("l1_projk_done")])
        t_wo[hc] = k.tok("vector")
        k.stage_guard[hc % 2] = t_wo[hc]
    xr_new = list(xr_toks)
    for dc in range(NCH):
        for s in range(NSL):
            sl = slice(s * SL, (s + 1) * SL)
            bid = "b%d" % (s % 2)
            bank = pb[s % 2]
            k.bank(bid, after=[t_wo[7]] + on_tok)
            for hc in range(8):
                k.on("tensor", lambda e, hc=hc, dc=dc, sl=sl, bank=bank: e.matmul(
                    bank[:], wo_bf[hc][:, dc * 128:(dc + 1) * 128],
                    o_n[hc][:, sl], start=(hc == 0), stop=(hc == 7)),
                    after=[t_wo[hc], on_tok[hc]])
            tmm = k.tok("tensor")
            k.on("vector", lambda e, dc=dc, sl=sl, bank=bank: e.tensor_tensor(
                xr[dc][:, sl], bank[:], xr[dc][:, sl], ALU.add),
                after=[tmm, xr_new[dc]])
            xr_new[dc] = k.tok("vector")
            k.bank_done(bid, xr_new[dc])
    if DEV_TAPS3:
        _taps2(k, dict(xhat0=xhat[0], qfm=qfm, kfm=kfm, vtm0=v_tm[0],
                       on0=o_n[0], on7=o_n[7], Sp=S_pos, ktm=ktm))
    return xr_new


def block_l1ffn(k, xr, xr_toks, scope):
    nc, b = k.nc, k.b
    pb = b["pb"]
    sb = scope
    QS = 1.0 / 127.0

    xhat = [sb(f"t3_xh{c}", (128, 2048), BF16) for c in range(NCH)]
    sub1 = contextlib.ExitStack()
    ss1 = lambda n, s, d: sub1.enter_context(nc.sbuf_tensor(n, list(s), d))
    sq = [ss1(f"t3_sq{i}", (128, 512), BF16) for i in range(2)]
    negm_rep = ss1("t3_negmrep", (128, 2048), BF16)
    rstd_rep = ss1("t3_rstdrep", (128, 2048), BF16)
    scratch = ss1("t3_scr", (128, 512), F32)

    xb_toks = []
    for c in range(NCH):
        k.on("scalar", lambda e, c=c: e.copy(xhat[c][:], xr[c][:]),
             after=[xr_toks[c]])
        xb_toks.append(k.tok("scalar"))
    st = emit_stats(k, xhat, xb_toks, sq)
    t_nm = emit_replicate(k, negm_rep, b["row_mean"], 4, [st["rows"]])
    t_rs = emit_replicate(k, rstd_rep, b["row_rstd"], 5, [st["rstd"]])
    xh_tok = [None] * NCH
    for c in range(NCH):
        for s in range(NSL):
            sl = slice(s * SL, (s + 1) * SL)
            k.on("vector", lambda e, c=c, sl=sl: e.tensor_tensor(
                scratch[:], xr[c][:, sl], negm_rep[:, sl], ALU.subtract),
                after=[t_nm, t_rs, st["rows"]])
            k.on("vector", lambda e, c=c, sl=sl: e.tensor_tensor(
                xhat[c][:, sl], scratch[:], rstd_rep[:, sl], ALU.mult))
            xh_tok[c] = k.tok("vector")
    sub1.close()

    h_half = [sb(f"t3_h{t}", (128, 1024), BF16) for t in range(24)]
    wg_bf = sb("t3_wg", (128, 1024), BF16)
    wu_bf = sb("t3_wu", (128, 1024), BF16)
    wd_bf = [sb(f"t3_wd{i}", (128, 512), BF16) for i in range(2)]
    silu_bf = sb("t3_silu", (128, 512), BF16)

    xr_new = list(xr_toks)
    for P in range(2):
        h_tok = [None] * 24
        for t in range(24):
            dg = None
            for c in range(NCH):
                dg = k.dma(b["wstage"][0][:].bitcast(I32)[:, c * 128:(c + 1) * 128],
                           b["in"]["l1f_wg"][t, c],
                           after=[k.stage_guard[0]], sem="w0")
            k.on("gpsimd", lambda e: e.tensor_copy(
                wg_bf[:], b["wstage"][0][:].bitcast(I32)[:]),
                after=[dg, k.b.get("l1f_wg_used")])
            t_wg = k.tok("gpsimd")
            k.stage_guard[0] = t_wg
            du = None
            for c in range(NCH):
                du = k.dma(b["wstage"][1][:].bitcast(I32)[:, c * 128:(c + 1) * 128],
                           b["in"]["l1f_wu"][t, c],
                           after=[k.stage_guard[1]], sem="w1")
            k.on("gpsimd", lambda e: e.tensor_copy(
                wu_bf[:], b["wstage"][1][:].bitcast(I32)[:]),
                after=[du, k.b.get("l1f_wu_used")])
            t_wu = k.tok("gpsimd")
            k.stage_guard[1] = t_wu
            for sl2 in range(2):
                gs = P * 2 + sl2
                sl = slice(gs * SL, (gs + 1) * SL)
                hsl = slice(sl2 * SL, (sl2 + 1) * SL)
                k.bank("g0", after=[t_wg] + k.b.get("l1f_prev_down", []))
                for c in range(NCH):
                    k.on("tensor", lambda e, c=c, sl=sl: e.matmul(
                        pb[0][:], wg_bf[:, c * 128:(c + 1) * 128],
                        xhat[c][:, sl], start=(c == 0), stop=(c == NCH - 1)),
                        after=[xh_tok[c]])
                tmm = k.tok("tensor")
                k.b["l1f_wg_used"] = tmm
                k.on("scalar", lambda e: e.activation(
                    silu_bf[:], pb[0][:], AF.Silu, bias=0.0, scale=QS),
                    after=[tmm, k.b.get("l1f_silu_used")])
                t_silu = k.tok("scalar")
                k.bank_done("g0", t_silu)
                k.bank("u0", after=[t_wu] + k.b.get("l1f_prev_down", []))
                for c in range(NCH):
                    k.on("tensor", lambda e, c=c, sl=sl: e.matmul(
                        pb[1][:], wu_bf[:, c * 128:(c + 1) * 128],
                        xhat[c][:, sl], start=(c == 0), stop=(c == NCH - 1)))
                tmm = k.tok("tensor")
                k.b["l1f_wu_used"] = tmm
                k.on("vector", lambda e, t=t, hsl=hsl: e.scalar_tensor_tensor(
                    h_half[t][:, hsl], silu_bf[:], QS, pb[1][:],
                    ALU.mult, ALU.mult), after=[tmm, t_silu])
                h_tok[t] = k.tok("vector")
                k.b["l1f_silu_used"] = h_tok[t]
                k.bank_done("u0", h_tok[t])
        down_evacs = []
        for grp in range(2):
            t_mm_last = None
            for t in range(24):
                par = t % 2
                dd = None
                for dci in range(4):
                    dd = k.dma(
                        b["wstage"][par][:].bitcast(I32)[:, dci * 128:(dci + 1) * 128],
                        b["in"]["l1f_wd"][grp * 4 + dci, t],
                        after=[k.stage_guard[par]], sem=f"w{par}")
                k.on("gpsimd", lambda e, par=par: e.tensor_copy(
                    wd_bf[par][:], b["wstage"][par][:].bitcast(I32)[:, 0:512]),
                    after=[dd, k.b.get("l1f_wd_used%d" % par)])
                t_wd = k.tok("gpsimd")
                k.stage_guard[par] = t_wd
                for dci in range(4):
                    for sl2 in range(2):
                        bid = "d%d" % (dci * 2 + sl2)
                        bank = pb[dci * 2 + sl2]
                        if t == 0:
                            k.bank(bid, after=[t_wd, h_tok[23]])
                        k.on("tensor", lambda e, t=t, dci=dci, sl2=sl2, bank=bank:
                             e.matmul(bank[:],
                                      wd_bf[t % 2][:, dci * 128:(dci + 1) * 128],
                                      h_half[t][:, sl2 * SL:(sl2 + 1) * SL],
                                      start=(t == 0), stop=(t == 23)),
                             after=[t_wd, h_tok[t]])
                t_mm_last = k.tok("tensor")
                k.b["l1f_wd_used%d" % par] = t_mm_last
            for dci in range(4):
                dc = grp * 4 + dci
                for sl2 in range(2):
                    gs = P * 2 + sl2
                    sl = slice(gs * SL, (gs + 1) * SL)
                    bid = "d%d" % (dci * 2 + sl2)
                    bank = pb[dci * 2 + sl2]
                    k.on("vector", lambda e, dc=dc, sl=sl, bank=bank:
                         e.scalar_tensor_tensor(
                             xr[dc][:, sl], bank[:], QS, xr[dc][:, sl],
                             ALU.mult, ALU.add),
                         after=[t_mm_last, xr_new[dc]])
                    xr_new[dc] = k.tok("vector")
                    k.bank_done(bid, xr_new[dc])
                    down_evacs.append(xr_new[dc])
        k.b["l1f_prev_down"] = down_evacs
    return xr_new


def build_program(n_blocks=None):
    if n_blocks is None:
        n_blocks = N_BLOCKS
    nc = bass.Bass("TRN2", target_bir_lowering=False, detect_race_conditions=False)
    stack = contextlib.ExitStack()

    ins = {}
    def IN(name, shape, dtype=F32):
        ins[name] = nc.dram_tensor(name, list(shape), dtype, kind="ExternalInput").ap()

    IN("x_fm", (1024, 2048))
    IN("l0_wq", (1024, 1024)); IN("l0_wk", (1024, 1024))
    IN("l0_wv", (1024, 1024)); IN("l0_wo", (1024, 1024))
    IN("l0_bq", (1, 1024)); IN("l0_bv", (1, 1024))
    IN("l0_bk_col", (128, 8)); IN("l0_bv_col", (128, 8)); IN("l0_bo_col", (128, 8))
    IN("l0_kp", (128, 256))
    IN("l0f_sim_t", (128, 128)); IN("l0f_gates", (1, 16))
    IN("l0f_qp", (8, 128, 1024)); IN("l0f_kp", (8, 128, 1024)); IN("l0f_vp", (8, 128, 1024))
    IN("l0f_op", (8, 128, 1024))
    IN("l1_wqT", (1024, 1024)); IN("l1_wkT", (1024, 1024))
    IN("l1_wvT", (1024, 1024)); IN("l1_woT", (1024, 1024))
    IN("l1_fmqwT", (128, 128)); IN("l1_fmkwT", (128, 128))
    IN("l1_fmqb_col", (128, 1)); IN("l1_fmkb_col", (128, 1)); IN("l1_fmkb_row", (1, 128))
    IN("l1_rmsw_col", (128, 1))
    IN("l1f_wg", (24, 8, 128, 128), I32)
    IN("l1f_wu", (24, 8, 128, 128), I32)
    IN("l1f_wd", (8, 24, 128, 128), I32)
    U16 = mybir.dt.uint16
    IN("c_ones_col_bf", (128, 1), U16); IN("c_ones_row_f32", (1, 128))
    IN("c_ones_row_bf", (1, 128), U16)
    IN("c_pair2_bf", (2, 128), U16); IN("c_fold64_bf", (128, 64), U16)
    IN("c_eye_bf", (128, 128), U16); IN("c_eye_f32", (128, 128))
    IN("c_invk", (128, 512))
    IN("c_cmask", (4, 128, 512), U16)
    IN("c_trimask", (128, 128), U16)

    out = nc.dram_tensor("y_fm", [1024, 2048], F32, kind="ExternalOutput").ap()

    k = K(nc, stack)
    b = k.b
    b["in"] = ins

    def SB(name, shape, dtype):
        return stack.enter_context(nc.sbuf_tensor(name, list(shape), dtype))

    xr = [SB(f"xr{c}", (128, 2048), F32) for c in range(NCH)]
    b["pb"] = [stack.enter_context(nc.psum_tensor(f"pb{i}", (128, 512), F32)) for i in range(8)]

    const_specs = [("ones_col_bf", (128, 1), BF16), ("ones_row_f32", (1, 128), F32),
                   ("ones_row_bf", (1, 128), BF16),
                   ("pair2_bf", (2, 128), BF16), ("fold64_bf", (128, 64), BF16),
                   ("eye_bf", (128, 128), BF16), ("eye_f32", (128, 128), F32),
                   ("trimask", (128, 128), BF16)]
    for nm, shape, dt in const_specs:
        b[nm] = SB("s_" + nm, shape, dt)
    b["eps_col"] = SB("eps_col", (128, 1), F32)
    b["row_mean"] = SB("row_mean", (1, 2048), F32)
    b["row_msvar"] = SB("row_msvar", (1, 2048), F32)
    b["row_rstd"] = SB("row_rstd", (1, 2048), F32)
    b["wstage"] = [SB(f"wstage{i}", (128, 1024), F32) for i in range(2)]

    for c in range(NCH):
        k.dma(xr[c][:], ins["x_fm"][c * 128:(c + 1) * 128, :])
    for nm, shape, dt in const_specs:
        srcap = ins["c_" + nm]
        if dt == BF16:
            srcap = srcap.bitcast(BF16)
        k.dma(b[nm][:], srcap)
    tlast = Tok("d_in", k.cnt["d_in"])
    xr_toks = [tlast] * NCH
    for e in ("tensor", "vector", "scalar", "gpsimd"):
        k._wait(e, [tlast])
    k.on("vector", lambda e: e.memset(b["eps_col"][:], 1e-5))
    k.tok("vector")

    with contextlib.ExitStack() as bs:
        def scope(name, shape, dtype):
            return bs.enter_context(nc.sbuf_tensor(name, list(shape), dtype))
        xr_toks = block_l0attn(k, xr, xr_toks, scope)
        k.barrier()
    if n_blocks >= 2:
        with contextlib.ExitStack() as bs:
            def scope(name, shape, dtype):
                return bs.enter_context(nc.sbuf_tensor(name, list(shape), dtype))
            xr_toks = block_l0ffn(k, xr, xr_toks, scope)
            k.barrier()
    if n_blocks >= 3:
        with contextlib.ExitStack() as bs:
            def scope(name, shape, dtype):
                return bs.enter_context(nc.sbuf_tensor(name, list(shape), dtype))
            xr_toks = block_l1attn(k, xr, xr_toks, scope)
            k.barrier()
    if n_blocks >= 4:
        with contextlib.ExitStack() as bs:
            def scope(name, shape, dtype):
                return bs.enter_context(nc.sbuf_tensor(name, list(shape), dtype))
            xr_toks = block_l1ffn(k, xr, xr_toks, scope)
            k.barrier()

    out_toks = [k.dma(out[c * 128:(c + 1) * 128, :], xr[c][:], after=[xr_toks[c]],
                      sem="out") for c in range(NCH)]
    k._wait("sync", [out_toks[-1]])

    stack.close()
    return nc


_NC_CACHE = {}


def kernel(**inputs):
    in_maps = [_prep_core(c, inputs) for c in range(8)]
    if N_BLOCKS not in _NC_CACHE:
        _NC_CACHE[N_BLOCKS] = build_program(N_BLOCKS)
    nc = _NC_CACHE[N_BLOCKS]
    res = run_bass_kernel_spmd(nc, in_maps, core_ids=list(range(8)))
    outs = [np.asarray(r["y_fm"]).T for r in res.results]
    return np.ascontiguousarray(np.stack(outs, axis=0))
